# version 06
# baseline (speedup 1.0000x reference)
import numpy as np
import ml_dtypes
import concourse.bass as bass
import concourse.mybir as mybir
from concourse.alu_op_type import AluOpType
from concourse.bass_utils import run_bass_kernel_spmd

F32 = mybir.dt.float32
BF16 = mybir.dt.bfloat16
AF = mybir.ActivationFunctionType
AX = mybir.AxisListType
BF = ml_dtypes.bfloat16
Alu = AluOpType

S = 2048
D = 2048
NSEQ = 2
NT = 16
EPS = 1e-5
NEG = -1.0e30
STAGE = "full"

ENGINES = ("sync", "tensor", "vector", "scalar", "gpsimd")
DMASEMS = ("sL0", "sL1", "sSX", "sOX", "sW", "sVS", "sST0", "sST1")
ESEM = {"tensor": "sT", "vector": "sV", "scalar": "sA", "gpsimd": "sG", "sync": "sY"}


class Prog:

    def __init__(self):
        self.q = {e: [] for e in ENGINES}
        self.cnt = {s: 0 for s in DMASEMS + tuple(set(ESEM.values()))}
        self.waited = {e: {} for e in ENGINES}
        self.writer = {}
        self.readers = {}

    def emit(self, eng, fn, reads=(), writes=(), sem=None, extra=(), hazard=False):
        if sem is None:
            sem = ESEM[eng]
        amt = 16 if sem in DMASEMS else 1
        self.cnt[sem] += amt
        tok = (sem, self.cnt[sem], "HZ" if hazard else eng, len(self.q[eng]))
        waits = list(extra)
        for r in reads:
            waits.append(self.writer.get(r))
        for w in writes:
            waits.append(self.writer.get(w))
            waits.extend(self.readers.get(w, ()))
        best = {}
        wd = self.waited[eng]
        nq = len(self.q[eng])
        for t in waits:
            if t is None:
                continue
            tsem, val, peng = t[0], t[1], t[2]
            qpos = t[3] if len(t) > 3 else -10**9
            if peng == eng and tsem not in DMASEMS and (nq - qpos - 1) >= 2:
                continue
            if wd.get(tsem, -1) >= val:
                continue
            if best.get(tsem, -1) < val:
                best[tsem] = val
        real = sorted(best.items())
        for tsem, val in real:
            wd[tsem] = val
        for r in reads:
            self.readers.setdefault(r, []).append(tok)
        for w in writes:
            self.writer[w] = tok
            self.readers[w] = []

        def run(e, semd):
            for tsem, val in real:
                e.wait_ge(semd[tsem], val)
            inst = fn(e)
            inst.then_inc(semd[sem], amt)

        self.q[eng].append(run)
        return tok

    def dma(self, out, in_, reads=(), writes=(), sem="sW", extra=(), slow=False):
        return self.emit("sync", lambda e: e.dma_start(out=out, in_=in_, allow_slow_non_contiguous=slow),
                         reads, writes, sem, extra)


def sap(ap_like, offset, dims):
    t = ap_like.tensor if hasattr(ap_like, "tensor") else ap_like
    return bass.AP(t, offset, dims)


def build_program():
    nc = bass.Bass("TRN2", target_bir_lowering=False)

    X0 = nc.dram_tensor("X0", [NSEQ, S, D], F32, kind="ExternalInput").ap()
    Wq_d = nc.dram_tensor("Wq", [128, D], F32, kind="ExternalInput").ap()
    Wk_d = nc.dram_tensor("Wk", [32, D], F32, kind="ExternalInput").ap()
    Wv_d = nc.dram_tensor("Wv", [32, D], F32, kind="ExternalInput").ap()
    Wo_d = nc.dram_tensor("Wo", [D, 128], F32, kind="ExternalInput").ap()
    Wf_d = nc.dram_tensor("Wf", [D, D], F32, kind="ExternalInput").ap()
    sw_d = nc.dram_tensor("sw", [D], F32, kind="ExternalInput").ap()
    bias_d = nc.dram_tensor("bias", [D], F32, kind="ExternalInput").ap()
    ve0_d = nc.dram_tensor("ve0", [128], F32, kind="ExternalInput").ap()
    ve1_d = nc.dram_tensor("ve1", [128], F32, kind="ExternalInput").ap()
    taur_d = nc.dram_tensor("taur", [128], F32, kind="ExternalInput").ap()
    idb_d = nc.dram_tensor("idb", [128, 128], BF16, kind="ExternalInput").ap()
    idf_d = nc.dram_tensor("idf", [128, 128], F32, kind="ExternalInput").ap()
    mask_d = nc.dram_tensor("maskt", [128, 128], F32, kind="ExternalInput").ap()
    onej_d = nc.dram_tensor("onej", [3, S], BF16, kind="ExternalInput").ap()
    irb_d = nc.dram_tensor("irb", [2, S], BF16, kind="ExternalInput").ap()
    bm16_d = nc.dram_tensor("bm16", [16, S], BF16, kind="ExternalInput").ap()
    hind_d = nc.dram_tensor("hind", [128, 16], BF16, kind="ExternalInput").ap()
    exp16_d = nc.dram_tensor("exp16", [16, 128], F32, kind="ExternalInput").ap()
    OUT = nc.dram_tensor("OUT", [NSEQ, S, D], F32, kind="ExternalOutput").ap()
    x1scr = nc.dram_tensor("x1scr", [S, D], F32, kind="Internal").ap()

    p = Prog()

    import contextlib
    with contextlib.ExitStack() as _st:
        def _e(cm):
            return _st.enter_context(cm)
        WT = _e(nc.sbuf_tensor("WT", [128, 16 * 2048], BF16))
        WQKV = _e(nc.sbuf_tensor("WQKV", [128, 16 * 192], BF16))
        WOD = _e(nc.sbuf_tensor("WOD", [128, 2048], BF16))
        xsq = _e(nc.sbuf_tensor("xsq", [128, 2048], F32))
        xt0 = _e(nc.sbuf_tensor("xt0", [128, 2048], F32))
        xt1 = _e(nc.sbuf_tensor("xt1", [128, 2048], F32))
        yb0 = _e(nc.sbuf_tensor("yb0", [128, 2048], BF16))
        yb1 = _e(nc.sbuf_tensor("yb1", [128, 2048], BF16))
        yT0 = _e(nc.sbuf_tensor("yT0", [128, 2048], BF16))
        yT1 = _e(nc.sbuf_tensor("yT1", [128, 2048], BF16))
        qkvst = _e(nc.sbuf_tensor("qkvst", [128, 384], BF16))
        qTp = _e(nc.sbuf_tensor("qTp", [128, 2080], BF16))
        kTp = _e(nc.sbuf_tensor("kTp", [32, 2080], BF16))
        vtm = _e(nc.sbuf_tensor("vtm", [128, 512], BF16))
        VST = _e(nc.sbuf_tensor("VST", [128, 640], BF16))
        qw = _e(nc.sbuf_tensor("qw", [128, 4096], BF16))
        kw = _e(nc.sbuf_tensor("kw", [128, 4096], BF16))
        eb0 = _e(nc.sbuf_tensor("eb0", [128, 512], BF16))
        eb1 = _e(nc.sbuf_tensor("eb1", [128, 512], BF16))
        stg0 = _e(nc.sbuf_tensor("stg0", [9, 512], BF16))
        stg1 = _e(nc.sbuf_tensor("stg1", [9, 512], BF16))
        atb = _e(nc.sbuf_tensor("atb", [128, 2048], BF16))
        atf = _e(nc.sbuf_tensor("atf", [128, 2048], BF16))
        denb = _e(nc.sbuf_tensor("denb", [16, 2048], BF16))
        denf = _e(nc.sbuf_tensor("denf", [16, 2048], F32))
        PNG0 = _e(nc.sbuf_tensor("PNG0", [16, 2112], BF16))
        PNG1 = _e(nc.sbuf_tensor("PNG1", [16, 2112], BF16))
        NM = _e(nc.sbuf_tensor("NM", [16, 2048], BF16))
        XTR3 = _e(nc.sbuf_tensor("XTR3", [3, 2048], BF16))
        idb_s = _e(nc.sbuf_tensor("idb_s", [128, 128], BF16))
        idf_s = _e(nc.sbuf_tensor("idf_s", [128, 128], F32))
        mask_s = _e(nc.sbuf_tensor("mask_s", [128, 128], F32))
        onej_s = _e(nc.sbuf_tensor("onej_s", [3, 2048], BF16))
        bm16_s = _e(nc.sbuf_tensor("bm16_s", [16, 2048], BF16))
        hind_s = _e(nc.sbuf_tensor("hind_s", [128, 16], BF16))
        exp16_s = _e(nc.sbuf_tensor("exp16_s", [16, 128], F32))
        hbrow = _e(nc.sbuf_tensor("hbrow", [1, 2048], BF16))
        w0row = _e(nc.sbuf_tensor("w0row", [1, 2048], BF16))
        sw16 = _e(nc.sbuf_tensor("sw16", [128, 16], F32))
        vesb = _e(nc.sbuf_tensor("vesb", [128, 3], F32))
        ve0b = _e(nc.sbuf_tensor("ve0b", [128, 1], BF16))
        taur_s = _e(nc.sbuf_tensor("taur_s", [128, 1], F32))
        stt = _e(nc.sbuf_tensor("stt", [128, 12], F32))
        stbank = _e(nc.sbuf_tensor("stbank", [128, 32], F32))
        psF0 = _e(nc.psum_tensor("psF0", [128, 512], F32))
        psF1 = _e(nc.psum_tensor("psF1", [128, 512], F32))
        psF2 = _e(nc.psum_tensor("psF2", [128, 512], F32))
        psF3 = _e(nc.psum_tensor("psF3", [128, 512], F32))
        psT0 = _e(nc.psum_tensor("psT0", [128, 512], F32))
        psT1 = _e(nc.psum_tensor("psT1", [128, 512], F32))
        psQ = _e(nc.psum_tensor("psQ", [128, 512], F32))
        semL0 = _e(nc.semaphore("sL0"))
        semL1 = _e(nc.semaphore("sL1"))
        semSX = _e(nc.semaphore("sSX"))
        semOX = _e(nc.semaphore("sOX"))
        semW = _e(nc.semaphore("sW"))
        semVS = _e(nc.semaphore("sVS"))
        semST0 = _e(nc.semaphore("sST0"))
        semST1 = _e(nc.semaphore("sST1"))
        semT = _e(nc.semaphore("sT"))
        semV = _e(nc.semaphore("sV"))
        semA = _e(nc.semaphore("sA"))
        semG = _e(nc.semaphore("sG"))
        semY = _e(nc.semaphore("sY"))
    if True:
        semd = {"sL0": semL0, "sL1": semL1, "sSX": semSX, "sOX": semOX, "sW": semW,
                "sVS": semVS, "sST0": semST0, "sST1": semST1,
                "sT": semT, "sV": semV, "sA": semA, "sG": semG, "sY": semY}
        psF = [psF0, psF1, psF2, psF3]
        psFn = ["psF0", "psF1", "psF2", "psF3"]
        psT = [psT0, psT1]
        psTn = ["psT0", "psT1"]
        xts = [xt0, xt1]
        xtn = ["xt0", "xt1"]
        ebs = [eb0, eb1]
        ebn = ["eb0", "eb1"]
        stgs = [stg0, stg1]
        stgn = ["stg0", "stg1"]
        ybs = [yb0, yb1]
        ybn = ["ybA", "ybB"]
        yTs = [yT0, yT1]
        yTn = [["yTA%d" % dc for dc in range(16)], ["yTB%d" % dc for dc in range(16)]]

        SSQ, SSUM, MU, VAR, ISG, NMS, TRCP, SBAR = range(8)

        p.dma(idb_s[:, :], idb_d, writes=["idb"])
        p.dma(idf_s[:, :], idf_d, writes=["idf"])
        p.dma(mask_s[:, :], mask_d, writes=["mask"])
        p.dma(onej_s[:, :], onej_d, writes=["onej"])
        p.dma(bm16_s[:, :], bm16_d, writes=["bm16"])
        p.dma(hind_s[:, :], hind_d, writes=["hind"])
        p.dma(exp16_s[:, :], exp16_d, writes=["exp16"])
        p.dma(XTR3[1:3, :], irb_d, writes=["XTR3a"])
        p.dma(sap(vesb, 0, [[3, 128], [1, 1]]), ve0_d, writes=["ve0"])
        p.dma(sap(vesb, 1, [[3, 128], [1, 1]]), ve1_d, writes=["ve1"])
        p.dma(sap(taur_s, 0, [[1, 128], [1, 1]]), taur_d, writes=["taur"])
        p.dma(sap(sw16, 0, [[16, 128], [1, 16]]),
              sap(sw_d, 0, [[1, 128], [128, 16]]), writes=["sw16"], slow=True)
        p.dma(sap(denf, 0, [[2048, 1], [1, 2048]]),
              sap(bias_d, 0, [[2048, 1], [1, 2048]]), writes=["denf"])

        p.emit("vector", lambda e: e.reciprocal(stt[:, TRCP:TRCP + 1], taur_s[:, 0:1]),
               reads=["taur"], writes=["stt"], hazard=True)
        p.emit("vector", lambda e: e.tensor_scalar(stt[:, SBAR:SBAR + 1], taur_s[:, 0:1], 16.0, None, Alu.mult),
               reads=["taur"], writes=["stt"])
        p.emit("vector", lambda e: e.reciprocal(stt[:, SBAR:SBAR + 1], stt[:, SBAR:SBAR + 1]),
               reads=["stt"], writes=["stt"], hazard=True)
        p.emit("vector", lambda e: e.tensor_tensor(vesb[:, 2:3], vesb[:, 1:2], vesb[:, 0:1], Alu.subtract),
               reads=["ve0", "ve1"], writes=["dv"])
        p.emit("vector", lambda e: e.tensor_scalar(hbrow[0:1, :], denf[0:1, :], 0.5, None, Alu.mult),
               reads=["denf"], writes=["hbrow"])
        p.emit("vector", lambda e: e.tensor_copy(ve0b[:, 0:1], vesb[:, 0:1]),
               reads=["ve0"], writes=["ve0b"])

        p.dma(sap(xsq, 0, [[2048, 128], [128, 16], [1, 128]]),
              sap(Wo_d, 0, [[128, 128], [16384, 16], [1, 128]]), writes=["xsq"])
        for oc in range(16):
            r = oc % 2
            p.emit("tensor", lambda e, oc=oc, r=r: e.matmul(psT[r][:, 0:128], xsq[:, oc * 128:(oc + 1) * 128], idf_s[:, :], start=True, stop=True),
                   reads=["xsq", "idf"], writes=[psTn[r]])
            p.emit("vector", lambda e, oc=oc, r=r: e.tensor_scalar(WOD[:, oc * 128:(oc + 1) * 128], psT[r][:, 0:128], vesb[:, 2:3], None, Alu.mult),
                   reads=[psTn[r], "dv"], writes=["WOD"])
            p.emit("vector", lambda e, oc=oc, r=r: e.tensor_copy(yT0[:, oc * 128:(oc + 1) * 128], psT[r][:, 0:128]),
                   reads=[psTn[r]], writes=[yTn[0][oc]])
        for ic in range(4):
            p.emit("tensor", lambda e, ic=ic: e.matmul(psQ[0:1, 0:512], ve0b[:, 0:1], yT0[:, ic * 512:(ic + 1) * 512], start=True, stop=True),
                   reads=["ve0b"] + [yTn[0][4 * ic + k] for k in range(4)], writes=["psQ"])
            p.emit("vector", lambda e, ic=ic: e.tensor_copy(w0row[0:1, ic * 512:(ic + 1) * 512], psQ[0:1, 0:512]),
                   reads=["psQ"], writes=["w0row"])

        for oc in range(16):
            p.dma(xsq[:, :], Wf_d[oc * 128:(oc + 1) * 128, :], writes=["xsq"])
            p.emit("vector", lambda e, oc=oc: e.tensor_scalar(yb0[:, :], xsq[:, :], sw16[:, oc:oc + 1], 0.5, Alu.mult, Alu.mult),
                   reads=["xsq", "sw16"], writes=["ybA"])
            for dc in range(16):
                r = dc % 2
                p.emit("tensor", lambda e, dc=dc, r=r: e.matmul(psT[r][:, 0:128], yb0[:, dc * 128:(dc + 1) * 128], idb_s[:, :], start=True, stop=True),
                       reads=["ybA", "idb"], writes=[psTn[r]])
                p.emit("vector", lambda e, oc=oc, dc=dc, r=r: e.tensor_copy(WT[:, dc * 2048 + oc * 128: dc * 2048 + (oc + 1) * 128], psT[r][:, 0:128]),
                       reads=[psTn[r]], writes=["WT"])

        p.dma(xsq[:, :], Wq_d, writes=["xsq"])
        for dc in range(16):
            r = dc % 2
            p.emit("tensor", lambda e, dc=dc, r=r: e.matmul(psT[r][:, 0:128], xsq[:, dc * 128:(dc + 1) * 128], idf_s[:, :], start=True, stop=True),
                   reads=["xsq", "idf"], writes=[psTn[r]])
            p.emit("vector", lambda e, dc=dc, r=r: e.tensor_copy(WQKV[:, dc * 192: dc * 192 + 128], psT[r][:, 0:128]),
                   reads=[psTn[r]], writes=["WQKV"])
        p.dma(xsq[0:32, :], Wk_d, writes=["xsq", "xsqk"])
        for dc in range(16):
            r = dc % 2
            p.emit("tensor", lambda e, dc=dc, r=r: e.matmul(psT[r][0:128, 0:32], xsq[0:32, dc * 128:(dc + 1) * 128], idf_s[0:32, 0:32], start=True, stop=True),
                   reads=["xsqk", "idf"], writes=[psTn[r]])
            p.emit("vector", lambda e, dc=dc, r=r: e.tensor_copy(WQKV[:, dc * 192 + 128: dc * 192 + 160], psT[r][0:128, 0:32]),
                   reads=[psTn[r]], writes=["WQKV"])
        p.dma(xsq[0:32, :], Wv_d, writes=["xsqk", "xsqv"])
        for dc in range(16):
            r = dc % 2
            p.emit("tensor", lambda e, dc=dc, r=r: e.matmul(psT[r][0:128, 0:32], xsq[0:32, dc * 128:(dc + 1) * 128], idf_s[0:32, 0:32], start=True, stop=True),
                   reads=["xsqv", "idf"], writes=[psTn[r]])
            p.emit("vector", lambda e, dc=dc, r=r: e.tensor_copy(WQKV[:, dc * 192 + 160: dc * 192 + 192], psT[r][0:128, 0:32]),
                   reads=[psTn[r]], writes=["WQKV"])

        p.emit("gpsimd", lambda e: e.memset(qTp[:, 0:31], 0.0), writes=["qTp_pad"])
        p.emit("gpsimd", lambda e: e.memset(kTp[:, 0:31], 0.0), writes=["kTp_pad"])

        def stats_chain(xres, xt, save_jt=None):
            p.emit("scalar", lambda e, xt=xt: e.activation(xsq[:, :], xt[:, :], AF.Square, accum_out=stt[:, SSQ:SSQ + 1]),
                   reads=[xres], writes=["xsq", "xsqk", "xsqv", "ssq"])
            p.emit("vector", lambda e, xt=xt: e.reduce_sum(stt[:, SSUM:SSUM + 1], xt[:, :], axis=AX.X),
                   reads=[xres], writes=["ssum"], hazard=True)
            p.emit("vector", lambda e: e.tensor_scalar(stt[:, MU:MU + 1], stt[:, SSUM:SSUM + 1], 1.0 / D, None, Alu.mult),
                   reads=["ssum"], writes=["mu"])
            p.emit("vector", lambda e: e.tensor_scalar(stt[:, NMS:NMS + 1], stt[:, MU:MU + 1], stt[:, MU:MU + 1], -EPS, Alu.mult, Alu.add),
                   reads=["mu"], writes=["nms"])
            p.emit("vector", lambda e: e.scalar_tensor_tensor(stt[:, VAR:VAR + 1], stt[:, SSQ:SSQ + 1], 1.0 / D, stt[:, NMS:NMS + 1], Alu.mult, Alu.subtract),
                   reads=["ssq", "nms"], writes=["var"])
            p.emit("scalar", lambda e: e.activation(stt[:, ISG:ISG + 1], stt[:, VAR:VAR + 1], AF.Sqrt),
                   reads=["var"], writes=["sig"])
            p.emit("vector", lambda e: e.reciprocal(stt[:, ISG:ISG + 1], stt[:, ISG:ISG + 1]),
                   reads=["sig"], writes=["isg"], hazard=True)
            if save_jt is not None:
                p.emit("vector", lambda e, jt=save_jt: e.tensor_copy(stbank[:, 2 * jt:2 * jt + 1], stt[:, MU:MU + 1]),
                       reads=["mu"], writes=["stbank"])
                p.emit("vector", lambda e, jt=save_jt: e.tensor_copy(stbank[:, 2 * jt + 1:2 * jt + 2], stt[:, ISG:ISG + 1]),
                       reads=["isg"], writes=["stbank"])

        def normalize(xres, xt, par):
            p.emit("vector", lambda e, xt=xt, par=par: e.tensor_scalar(ybs[par][:, :], xt[:, :], stt[:, MU:MU + 1], stt[:, ISG:ISG + 1], Alu.subtract, Alu.mult),
                   reads=[xres, "mu", "isg"], writes=[ybn[par]])

        def transposes16(par):
            for dc in range(16):
                r = dc % 2
                p.emit("tensor", lambda e, dc=dc, r=r, par=par: e.matmul(psT[r][:, 0:128], ybs[par][:, dc * 128:(dc + 1) * 128], idb_s[:, :], start=True, stop=True),
                       reads=[ybn[par], "idb"], writes=[psTn[r]])
                p.emit("vector", lambda e, dc=dc, r=r, par=par: e.tensor_copy(yTs[par][:, dc * 128:(dc + 1) * 128], psT[r][:, 0:128]),
                       reads=[psTn[r]], writes=[yTn[par][dc]])

        def ffn_mms(par):
            for dc in range(16):
                def mmgrp(e, dc=dc, par=par):
                    inst = None
                    for o4 in range(4):
                        inst = e.matmul(psF[o4][:, 0:512], yTs[par][:, dc * 128:(dc + 1) * 128],
                                        WT[:, dc * 2048 + o4 * 512: dc * 2048 + (o4 + 1) * 512],
                                        start=(dc == 0), stop=False)
                    return inst
                p.emit("tensor", mmgrp, reads=[yTn[par][dc], "WT"], writes=psFn)

            def mmbias(e):
                inst = None
                for o4 in range(4):
                    inst = e.matmul(psF[o4][:, 0:512], onej_s[0:1, 0:128],
                                    hbrow[0:1, o4 * 512:(o4 + 1) * 512], start=False, stop=True)
                return inst
            p.emit("tensor", mmbias, reads=["onej", "hbrow"], writes=psFn)

        def evac_residual(xres, xt):
            for o4 in range(4):
                p.emit("vector", lambda e, o4=o4, xt=xt: e.tensor_tensor(xt[:, o4 * 512:(o4 + 1) * 512], xt[:, o4 * 512:(o4 + 1) * 512], psF[o4][:, 0:512], Alu.add),
                       reads=[psFn[o4], xres], writes=[xres])

        def emit_pv(g, h, prev):
            r, off, wdt, jt, ic, last = prev
            p.emit("tensor", lambda e, r=r, off=off, wdt=wdt, jt=jt, g=g: e.matmul(
                psQ[0:9, off:512], VST[:, jt * 40 + g * 9: jt * 40 + g * 9 + 9], ebs[r][:, 0:wdt],
                start=(jt == 0), stop=last),
                   reads=["VST", ebn[r]], writes=["psQ"])
            if last:
                sb = ic % 2
                p.emit("vector", lambda e, sb=sb: e.tensor_copy(stgs[sb][0:9, 0:512], psQ[0:9, 0:512]),
                       reads=["psQ"], writes=[stgn[sb]])
                p.dma(atb[h * 8:(h + 1) * 8, ic * 512:(ic + 1) * 512], stgs[sb][0:8, 0:512],
                      reads=[stgn[sb]], writes=["atb%d_%d" % (h, ic)], sem="sST%d" % sb)
                p.dma(denb[h:h + 1, ic * 512:(ic + 1) * 512], stgs[sb][8:9, 0:512],
                      reads=[stgn[sb]], writes=["denb%d_%d" % (h, ic)], sem="sST%d" % sb)

        for s_ in range(NSEQ):
            s = s_

            def preA(jt):
                b = jt % 2
                xt, xres = xts[b], xtn[b]
                p.dma(xt[:, :], X0[s, jt * 128:(jt + 1) * 128, :], writes=[xres], sem="sL%d" % b)
                stats_chain(xres, xt)
                normalize(xres, xt, b)
                transposes16(b)

            preA(0)
            for jt in range(NT):
                if jt + 1 < NT:
                    preA(jt + 1)
                b = jt % 2
                xt, xres = xts[b], xtn[b]
                ffn_mms(b)
                evac_residual(xres, xt)
                p.emit("sync", lambda e, jt=jt, xt=xt: e.dma_start(out=x1scr[jt * 128:(jt + 1) * 128, :], in_=xt[:, :]),
                       reads=[xres], writes=["x1scr%d" % jt], sem="sSX")
                stats_chain(xres, xt, save_jt=jt)

            def preB(jt):
                b = jt % 2
                xt, xres = xts[b], xtn[b]
                p.dma(xt[:, :], x1scr[jt * 128:(jt + 1) * 128, :],
                      reads=["x1scr%d" % jt], writes=[xres], sem="sL%d" % b,
                      extra=[("sSX", p.cnt["sSX"], "x")])
                p.emit("vector", lambda e, jt=jt, xt=xt, b=b: e.tensor_scalar(ybs[b][:, :], xt[:, :], stbank[:, 2 * jt:2 * jt + 1], stbank[:, 2 * jt + 1:2 * jt + 2], Alu.subtract, Alu.mult),
                       reads=[xres, "stbank"], writes=[ybn[b]])
                transposes16(b)

            def qkvB(jt):
                b = jt % 2
                co = 192 * b
                for dc in range(16):
                    p.emit("tensor", lambda e, dc=dc, b=b: e.matmul(psQ[:, 0:192], yTs[b][:, dc * 128:(dc + 1) * 128], WQKV[:, dc * 192:(dc + 1) * 192], start=(dc == 0), stop=(dc == 15)),
                           reads=[yTn[b][dc], "WQKV"], writes=["psQ"])
                p.emit("scalar", lambda e, co=co: e.activation(qkvst[:, co:co + 160], psQ[:, 0:160], AF.Tanh, scale=stt[:, TRCP:TRCP + 1]),
                       reads=["psQ", "stt"], writes=["qk%d" % b])
                p.emit("scalar", lambda e, co=co: e.activation(qkvst[:, co + 160:co + 192], psQ[:, 160:192], AF.Sigmoid, scale=stt[:, TRCP:TRCP + 1]),
                       reads=["psQ", "stt"], writes=["vv%d" % b])

            def qkTB(jt):
                b = jt % 2
                co = 192 * b
                p.emit("tensor", lambda e, co=co, r=b: e.matmul(psT[r][:, 0:128], qkvst[:, co:co + 128], idb_s[:, :], start=True, stop=True),
                       reads=["qk%d" % b, "idb"], writes=[psTn[b]])
                p.emit("vector", lambda e, jt=jt, r=b: e.tensor_copy(qTp[:, 31 + jt * 128: 31 + (jt + 1) * 128], psT[r][:, 0:128]),
                       reads=[psTn[b]], writes=["qTp"])
                p.emit("tensor", lambda e, co=co, r=b: e.matmul(psT[1 - r][0:32, 0:128], qkvst[:, co + 128:co + 160], idb_s[:, :], start=True, stop=True),
                       reads=["qk%d" % b, "idb"], writes=[psTn[1 - b]])
                p.emit("vector", lambda e, jt=jt, r=b: e.tensor_copy(kTp[0:32, 31 + jt * 128: 31 + (jt + 1) * 128], psT[1 - r][0:32, 0:128]),
                       reads=[psTn[1 - b]], writes=["kTp"])
                p.emit("vector", lambda e, jt=jt, co=co: e.tensor_copy(vtm[:, jt * 32:(jt + 1) * 32], qkvst[:, co + 160:co + 192]),
                       reads=["vv%d" % b], writes=["vtm"])

            preB(0)
            qkvB(0)
            for jt in range(NT):
                if jt + 1 < NT:
                    preB(jt + 1)
                    qkvB(jt + 1)
                qkTB(jt)

            p.emit("scalar", lambda e: e.activation(yb0[:, :], qTp[:, 31:2079], AF.Abs),
                   reads=["qTp", "qTp_pad"], writes=["ybA"])
            p.emit("gpsimd", lambda e: e.memset(PNG0[:, 0:31], 0.0), writes=["PNG0"])
            p.emit("gpsimd", lambda e: e.memset(PNG0[:, 2079:2112], 0.0), writes=["PNG0"])
            p.emit("gpsimd", lambda e: e.memset(PNG1[:, 2079:2112], 0.0), writes=["PNG1"])
            for ic in range(4):
                p.emit("tensor", lambda e, ic=ic: e.matmul(psQ[0:16, 0:512], hind_s[:, :], yb0[:, ic * 512:(ic + 1) * 512], start=True, stop=True),
                       reads=["ybA", "hind"], writes=["psQ"])
                p.emit("vector", lambda e, ic=ic: e.tensor_copy(PNG0[0:16, 31 + ic * 512: 31 + (ic + 1) * 512], psQ[0:16, 0:512]),
                       reads=["psQ"], writes=["PNG0"])
            cur, oth = (PNG0, "PNG0"), (PNG1, "PNG1")
            for sh in (1, 2, 4, 8, 16):
                p.emit("vector", lambda e, cur=cur, oth=oth, sh=sh: e.tensor_tensor(oth[0][:, 0:2080], cur[0][:, 0:2080], cur[0][:, sh:2080 + sh], Alu.add),
                       reads=[cur[1]], writes=[oth[1]])
                cur, oth = oth, cur
            p.emit("vector", lambda e, cur=cur: e.scalar_tensor_tensor(NM[0:16, :], cur[0][0:16, 0:2048], -1.0, bm16_s[0:16, :], Alu.mult, Alu.subtract),
                   reads=[cur[1], "bm16"], writes=["NM"])

            p.emit("gpsimd", lambda e: e.memset(VST[:, :], 1.0), writes=["VST"])
            for g in range(4):
                p.dma(sap(VST, g * 9, [[640, 127], [40, 16], [1, 8]]),
                      sap(vtm, 512 + g * 8, [[512, 127], [32, 16], [1, 8]]),
                      reads=["vtm"], writes=["VST"], sem="sVS")
                p.dma(sap(VST, 127 * 640 + g * 9, [[640, 1], [40, 15], [1, 8]]),
                      sap(vtm, 32 + g * 8, [[512, 1], [32, 15], [1, 8]]),
                      reads=["vtm"], writes=["VST"], sem="sVS")
            for g in range(4):
                p.dma(sap(VST, 127 * 640 + 600 + g * 9, [[640, 1], [1, 8]]),
                      sap(qTp, 127 * 2080, [[2080, 1], [1, 8]]),
                      reads=["qTp_pad"], writes=["VST"], sem="sVS")

            tiles = [(ic, jt) for ic in range(4) for jt in range(min(4 * ic + 3, NT - 1) + 1)]
            for g in range(4):
                for c in range(2):
                    p.dma(sap(kw, c * 2048, [[4096, 128], [1, 2048]]),
                          sap(kTp, (g * 8) * 2080 + c * 16, [[2080, 8], [1, 16], [1, 2048]]),
                          reads=["kTp", "kTp_pad"], writes=["kw%d" % c])
                for h in range(4 * g, 4 * g + 4):
                    for c in range(2):
                        p.dma(sap(qw, c * 2048, [[4096, 128], [1, 2048]]),
                              sap(qTp, (h * 8) * 2080 + c * 16, [[2080, 8], [1, 16], [1, 2048]]),
                              reads=["qTp", "qTp_pad"], writes=["qw%d" % c])
                    p.dma(XTR3[0:1, :], NM[h:h + 1, :], reads=["NM"], writes=["XTR3"])
                    prev = None
                    for n, (ic, jt) in enumerate(tiles):
                        i0 = max(ic * 512, jt * 128)
                        off = i0 - ic * 512
                        wdt = 512 - off
                        last = (jt == min(4 * ic + 3, NT - 1))
                        r = n % 2

                        def mmsc(e, r=r, jt=jt, i0=i0, wdt=wdt):
                            e.matmul(psT[r][:, 0:wdt], kw[:, jt * 128:(jt + 1) * 128], qw[:, i0:i0 + wdt], start=True, stop=False)
                            e.matmul(psT[r][:, 0:wdt], kw[:, 2048 + jt * 128: 2048 + (jt + 1) * 128], qw[:, 2048 + i0: 2048 + i0 + wdt], start=False, stop=False)
                            return e.matmul(psT[r][:, 0:wdt], onej_s[0:3, jt * 128:(jt + 1) * 128], XTR3[0:3, i0:i0 + wdt], start=False, stop=True)
                        p.emit("tensor", mmsc, reads=["kw0", "kw1", "qw0", "qw1", "onej", "XTR3", "XTR3a"], writes=[psTn[r]])
                        if jt >= 4 * ic:
                            p.emit("vector", lambda e, r=r: e.tensor_tensor(psT[r][:, 0:128], psT[r][:, 0:128], mask_s[:, :], Alu.add),
                                   reads=["mask"], writes=[psTn[r]])
                        p.emit("scalar", lambda e, r=r, wdt=wdt: e.activation(ebs[r][:, 0:wdt], psT[r][:, 0:wdt], AF.Exp, scale=stt[:, SBAR:SBAR + 1]),
                               reads=[psTn[r], "stt"], writes=[ebn[r]])
                        if prev is not None:
                            emit_pv(g, h, prev)
                        prev = (r, off, wdt, jt, ic, last)
                    emit_pv(g, h, prev)

            all_atb = ["atb%d_%d" % (h, ic) for h in range(16) for ic in range(4)]
            all_den = ["denb%d_%d" % (h, ic) for h in range(16) for ic in range(4)]
            p.emit("vector", lambda e: e.tensor_scalar(denf[:, :], denb[:, :], 1e-37, None, Alu.add),
                   reads=all_den, writes=["denf"])
            p.emit("vector", lambda e: e.reciprocal(denf[:, :], denf[:, :]),
                   reads=["denf"], writes=["denf"], hazard=True)
            for ic in range(4):
                p.emit("tensor", lambda e, ic=ic: e.matmul(psF[ic][:, 0:512], exp16_s[:, :], denf[0:16, ic * 512:(ic + 1) * 512], start=True, stop=True),
                       reads=["denf", "exp16"], writes=[psFn[ic]])
                p.emit("vector", lambda e, ic=ic: e.tensor_tensor(atf[:, ic * 512:(ic + 1) * 512], atb[:, ic * 512:(ic + 1) * 512], psF[ic][:, 0:512], Alu.mult),
                       reads=all_atb + [psFn[ic]], writes=["atf"])

            prB = [psQ, psT0, psT1, psQ]
            prBn = ["psQ", "psT0", "psT1", "psQ"]

            def preC(jt):
                b = jt % 2
                xt, xres = xts[b], xtn[b]
                p.dma(xt[:, :], x1scr[jt * 128:(jt + 1) * 128, :],
                      reads=["x1scr%d" % jt], writes=[xres], sem="sL%d" % b,
                      extra=[("sSX", p.cnt["sSX"], "x")])
                def pj(e, jt=jt, o4s=(0, 1, 2)):
                    inst = None
                    for o4 in o4s:
                        inst = e.matmul(prB[o4][:, 0:512], atf[:, jt * 128:(jt + 1) * 128], WOD[:, o4 * 512:(o4 + 1) * 512], start=True, stop=False)
                    for o4 in o4s:
                        inst = e.matmul(prB[o4][:, 0:512], onej_s[0:1, jt * 128:(jt + 1) * 128], w0row[0:1, o4 * 512:(o4 + 1) * 512], start=False, stop=True)
                    return inst
                p.emit("tensor", lambda e, jt=jt: pj(e, jt, (0, 1, 2)), reads=["atf", "WOD", "onej", "w0row"], writes=["psQ", "psT0", "psT1"])
                for o4 in (0, 1, 2):
                    p.emit("vector", lambda e, o4=o4, xt=xt: e.tensor_tensor(xt[:, o4 * 512:(o4 + 1) * 512], xt[:, o4 * 512:(o4 + 1) * 512], prB[o4][:, 0:512], Alu.add),
                           reads=[prBn[o4], xres], writes=[xres])
                p.emit("tensor", lambda e, jt=jt: pj(e, jt, (3,)), reads=["atf", "WOD", "onej", "w0row"], writes=["psQ"])
                p.emit("vector", lambda e, xt=xt: e.tensor_tensor(xt[:, 3 * 512:4 * 512], xt[:, 3 * 512:4 * 512], psQ[:, 0:512], Alu.add),
                       reads=["psQ", xres], writes=[xres])
                stats_chain(xres, xt)
                normalize(xres, xt, b)
                transposes16(b)

            preC(0)
            for jt in range(NT):
                if jt + 1 < NT:
                    preC(jt + 1)
                b = jt % 2
                xt, xres = xts[b], xtn[b]
                ffn_mms(b)
                evac_residual(xres, xt)
                p.emit("sync", lambda e, s=s, jt=jt, xt=xt: e.dma_start(out=OUT[s, jt * 128:(jt + 1) * 128, :], in_=xt[:, :]),
                       reads=[xres], writes=["out%d_%d" % (s, jt)], sem="sOX")

        p.emit("sync", lambda e: e.nop(),
               extra=[(sm, p.cnt[sm], "x") for sm in DMASEMS])

        with nc.Block() as blk:
            @blk.sync
            def _(eng):
                for f in p.q["sync"]:
                    f(eng, semd)

            @blk.tensor
            def _(eng):
                for f in p.q["tensor"]:
                    f(eng, semd)

            @blk.vector
            def _(eng):
                for f in p.q["vector"]:
                    f(eng, semd)

            @blk.scalar
            def _(eng):
                for f in p.q["scalar"]:
                    f(eng, semd)

            @blk.gpsimd
            def _(eng):
                for f in p.q["gpsimd"]:
                    f(eng, semd)

    return nc


def _consts():
    j = np.arange(S, dtype=np.float64)
    jhi = (j.astype(np.int64) & ~255).astype(np.float64)
    jlo = (j.astype(np.int64) & 255).astype(np.float64)
    onej = np.stack([np.ones(S), jhi, jlo]).astype(BF)
    ir = 16.0 / (j + 1.0)
    irb = np.stack([ir, ir]).astype(BF)
    bm16 = np.broadcast_to(16.0 * (j - 1.0) / (j + 1.0), (16, S)).astype(BF)
    idb = np.eye(128, dtype=BF)
    idf = np.eye(128, dtype=np.float32)
    pp, cc = np.mgrid[0:128, 0:128]
    maskt = np.where(pp >= cc, np.float32(NEG), np.float32(0.0)).astype(np.float32)
    hind = np.zeros((128, 16), dtype=BF)
    for h in range(16):
        hind[h * 8:(h + 1) * 8, h] = 1
    exp16 = np.zeros((16, 128), dtype=np.float32)
    for h in range(16):
        exp16[h, h * 8:(h + 1) * 8] = 1
    return dict(onej=onej, irb=irb, bm16=bm16, idb=idb, idf=idf, maskt=maskt,
                hind=hind, exp16=exp16)


def kernel(L0_attn_hidden_states, L0_attn_Wq, L0_attn_Wk, L0_attn_Wv, L0_attn_Wo,
           L0_attn_v_emb0, L0_attn_v_emb1, L0_attn_tau,
           L0_ffn_weight, L0_ffn_scale_weight, L0_ffn_bias):
    x = np.ascontiguousarray(np.asarray(L0_attn_hidden_states, dtype=np.float32))
    consts = _consts()
    taur = np.broadcast_to(np.asarray(L0_attn_tau, dtype=np.float32).reshape(1), (128,)).copy()
    shared = dict(
        Wq=np.ascontiguousarray(L0_attn_Wq, dtype=np.float32),
        Wk=np.ascontiguousarray(L0_attn_Wk, dtype=np.float32),
        Wv=np.ascontiguousarray(L0_attn_Wv, dtype=np.float32),
        Wo=np.ascontiguousarray(L0_attn_Wo, dtype=np.float32),
        Wf=np.ascontiguousarray(L0_ffn_weight, dtype=np.float32),
        sw=np.ascontiguousarray(L0_ffn_scale_weight, dtype=np.float32),
        bias=np.ascontiguousarray(L0_ffn_bias, dtype=np.float32),
        ve0=np.ascontiguousarray(L0_attn_v_emb0, dtype=np.float32),
        ve1=np.ascontiguousarray(L0_attn_v_emb1, dtype=np.float32),
        taur=taur,
        **consts,
    )
    nc = build_program()
    in_maps = []
    for c in range(8):
        m = dict(shared)
        m["X0"] = x[c * NSEQ:(c + 1) * NSEQ]
        in_maps.append(m)
    res = run_bass_kernel_spmd(nc, in_maps, core_ids=list(range(8)))
    out = np.concatenate([r["OUT"] for r in res.results], axis=0)
    return out




# version 07
# speedup vs baseline: 1.1027x; 1.1027x over previous
import numpy as np
import ml_dtypes
import concourse.bass as bass
import concourse.mybir as mybir
from concourse.alu_op_type import AluOpType
from concourse.bass_utils import run_bass_kernel_spmd

F32 = mybir.dt.float32
BF16 = mybir.dt.bfloat16
AF = mybir.ActivationFunctionType
AX = mybir.AxisListType
BF = ml_dtypes.bfloat16
Alu = AluOpType

S = 2048
D = 2048
NSEQ = 2
NT = 16
EPS = 1e-5
NEG = -1.0e30
STAGE = "full"

ENGINES = ("sync", "tensor", "vector", "scalar", "gpsimd")
DMASEMS = ("sL0", "sL1", "sSX", "sOX", "sW", "sVS", "sST0", "sST1")
ESEM = {"tensor": "sT", "vector": "sV", "scalar": "sA", "gpsimd": "sG", "sync": "sY"}


class Prog:

    def __init__(self):
        self.q = {e: [] for e in ENGINES}
        self.cnt = {s: 0 for s in DMASEMS + tuple(set(ESEM.values()))}
        self.waited = {e: {} for e in ENGINES}
        self.writer = {}
        self.readers = {}

    def emit(self, eng, fn, reads=(), writes=(), sem=None, extra=(), hazard=False):
        if sem is None:
            sem = ESEM[eng]
        amt = 16 if sem in DMASEMS else 1
        self.cnt[sem] += amt
        tok = (sem, self.cnt[sem], "HZ" if hazard else eng, len(self.q[eng]))
        waits = list(extra)
        for r in reads:
            waits.append(self.writer.get(r))
        for w in writes:
            waits.append(self.writer.get(w))
            waits.extend(self.readers.get(w, ()))
        best = {}
        wd = self.waited[eng]
        nq = len(self.q[eng])
        for t in waits:
            if t is None:
                continue
            tsem, val, peng = t[0], t[1], t[2]
            qpos = t[3] if len(t) > 3 else -10**9
            if peng == eng and tsem not in DMASEMS and (nq - qpos - 1) >= 2:
                continue
            if wd.get(tsem, -1) >= val:
                continue
            if best.get(tsem, -1) < val:
                best[tsem] = val
        real = sorted(best.items())
        for tsem, val in real:
            wd[tsem] = val
        for r in reads:
            self.readers.setdefault(r, []).append(tok)
        for w in writes:
            self.writer[w] = tok
            self.readers[w] = []

        def run(e, semd):
            for tsem, val in real:
                e.wait_ge(semd[tsem], val)
            inst = fn(e)
            inst.then_inc(semd[sem], amt)

        self.q[eng].append(run)
        return tok

    def dma(self, out, in_, reads=(), writes=(), sem="sW", extra=(), slow=False):
        return self.emit("sync", lambda e: e.dma_start(out=out, in_=in_, allow_slow_non_contiguous=slow),
                         reads, writes, sem, extra)


def sap(ap_like, offset, dims):
    t = ap_like.tensor if hasattr(ap_like, "tensor") else ap_like
    return bass.AP(t, offset, dims)


def build_program():
    nc = bass.Bass("TRN2", target_bir_lowering=False)

    X0 = nc.dram_tensor("X0", [NSEQ, S, D], F32, kind="ExternalInput").ap()
    Wq_d = nc.dram_tensor("Wq", [128, D], F32, kind="ExternalInput").ap()
    Wk_d = nc.dram_tensor("Wk", [32, D], F32, kind="ExternalInput").ap()
    Wv_d = nc.dram_tensor("Wv", [32, D], F32, kind="ExternalInput").ap()
    Wo_d = nc.dram_tensor("Wo", [D, 128], F32, kind="ExternalInput").ap()
    Wf_d = nc.dram_tensor("Wf", [D, D], F32, kind="ExternalInput").ap()
    sw_d = nc.dram_tensor("sw", [D], F32, kind="ExternalInput").ap()
    bias_d = nc.dram_tensor("bias", [D], F32, kind="ExternalInput").ap()
    ve0_d = nc.dram_tensor("ve0", [128], F32, kind="ExternalInput").ap()
    ve1_d = nc.dram_tensor("ve1", [128], F32, kind="ExternalInput").ap()
    taur_d = nc.dram_tensor("taur", [128], F32, kind="ExternalInput").ap()
    idb_d = nc.dram_tensor("idb", [128, 128], BF16, kind="ExternalInput").ap()
    idf_d = nc.dram_tensor("idf", [128, 128], F32, kind="ExternalInput").ap()
    mask_d = nc.dram_tensor("maskt", [128, 128], F32, kind="ExternalInput").ap()
    onej_d = nc.dram_tensor("onej", [3, S], BF16, kind="ExternalInput").ap()
    irb_d = nc.dram_tensor("irb", [2, S], BF16, kind="ExternalInput").ap()
    bm16_d = nc.dram_tensor("bm16", [16, S], BF16, kind="ExternalInput").ap()
    hind_d = nc.dram_tensor("hind", [128, 16], BF16, kind="ExternalInput").ap()
    exp16_d = nc.dram_tensor("exp16", [16, 128], F32, kind="ExternalInput").ap()
    OUT = nc.dram_tensor("OUT", [NSEQ, S, D], F32, kind="ExternalOutput").ap()
    x1scr = nc.dram_tensor("x1scr", [S, D], F32, kind="Internal").ap()

    p = Prog()

    import contextlib
    with contextlib.ExitStack() as _st:
        def _e(cm):
            return _st.enter_context(cm)
        WT = _e(nc.sbuf_tensor("WT", [128, 16 * 2048], BF16))
        WQKV = _e(nc.sbuf_tensor("WQKV", [128, 16 * 192], BF16))
        WOD = _e(nc.sbuf_tensor("WOD", [128, 2048], BF16))
        xsq = _e(nc.sbuf_tensor("xsq", [128, 2048], F32))
        xt0 = _e(nc.sbuf_tensor("xt0", [128, 2048], F32))
        xt1 = _e(nc.sbuf_tensor("xt1", [128, 2048], F32))
        yb0 = _e(nc.sbuf_tensor("yb0", [128, 2048], BF16))
        yb1 = _e(nc.sbuf_tensor("yb1", [128, 2048], BF16))
        yT0 = _e(nc.sbuf_tensor("yT0", [128, 2048], BF16))
        yT1 = _e(nc.sbuf_tensor("yT1", [128, 2048], BF16))
        qkvst = _e(nc.sbuf_tensor("qkvst", [128, 384], BF16))
        qTp = _e(nc.sbuf_tensor("qTp", [128, 2080], BF16))
        kTp = _e(nc.sbuf_tensor("kTp", [32, 2080], BF16))
        vtm = _e(nc.sbuf_tensor("vtm", [128, 512], BF16))
        VST = _e(nc.sbuf_tensor("VST", [128, 640], BF16))
        qw = _e(nc.sbuf_tensor("qw", [128, 4096], BF16))
        qw2 = _e(nc.sbuf_tensor("qw2", [128, 4096], BF16))
        kw = _e(nc.sbuf_tensor("kw", [128, 4096], BF16))
        kw2 = _e(nc.sbuf_tensor("kw2", [128, 4096], BF16))
        eb0 = _e(nc.sbuf_tensor("eb0", [128, 512], BF16))
        eb1 = _e(nc.sbuf_tensor("eb1", [128, 512], BF16))
        stg0 = _e(nc.sbuf_tensor("stg0", [9, 512], BF16))
        stg1 = _e(nc.sbuf_tensor("stg1", [9, 512], BF16))
        atb = _e(nc.sbuf_tensor("atb", [128, 2048], BF16))
        atf = _e(nc.sbuf_tensor("atf", [128, 2048], BF16))
        denb = _e(nc.sbuf_tensor("denb", [16, 2048], BF16))
        denf = xsq
        PNG0 = _e(nc.sbuf_tensor("PNG0", [16, 2112], BF16))
        PNG1 = _e(nc.sbuf_tensor("PNG1", [16, 2112], BF16))
        NM = PNG0
        XTR3 = _e(nc.sbuf_tensor("XTR3", [3, 2048], BF16))
        XTR3b = _e(nc.sbuf_tensor("XTR3b", [3, 2048], BF16))
        idb_s = _e(nc.sbuf_tensor("idb_s", [128, 128], BF16))
        idf_s = _e(nc.sbuf_tensor("idf_s", [128, 128], F32))
        mask_s = _e(nc.sbuf_tensor("mask_s", [128, 128], F32))
        onej_s = _e(nc.sbuf_tensor("onej_s", [3, 2048], BF16))
        bm16_s = _e(nc.sbuf_tensor("bm16_s", [16, 2048], BF16))
        hind_s = _e(nc.sbuf_tensor("hind_s", [128, 16], BF16))
        exp16_s = _e(nc.sbuf_tensor("exp16_s", [16, 128], F32))
        hbrow = _e(nc.sbuf_tensor("hbrow", [1, 2048], BF16))
        w0row = _e(nc.sbuf_tensor("w0row", [1, 2048], BF16))
        sw16 = _e(nc.sbuf_tensor("sw16", [128, 16], F32))
        vesb = _e(nc.sbuf_tensor("vesb", [128, 3], F32))
        ve0b = _e(nc.sbuf_tensor("ve0b", [128, 1], BF16))
        taur_s = _e(nc.sbuf_tensor("taur_s", [128, 1], F32))
        stt = _e(nc.sbuf_tensor("stt", [128, 12], F32))
        stbank = _e(nc.sbuf_tensor("stbank", [128, 32], F32))
        psF0 = _e(nc.psum_tensor("psF0", [128, 512], F32))
        psF1 = _e(nc.psum_tensor("psF1", [128, 512], F32))
        psF2 = _e(nc.psum_tensor("psF2", [128, 512], F32))
        psF3 = _e(nc.psum_tensor("psF3", [128, 512], F32))
        psT0 = _e(nc.psum_tensor("psT0", [128, 512], F32))
        psT1 = _e(nc.psum_tensor("psT1", [128, 512], F32))
        psQ = _e(nc.psum_tensor("psQ", [128, 512], F32))
        semL0 = _e(nc.semaphore("sL0"))
        semL1 = _e(nc.semaphore("sL1"))
        semSX = _e(nc.semaphore("sSX"))
        semOX = _e(nc.semaphore("sOX"))
        semW = _e(nc.semaphore("sW"))
        semVS = _e(nc.semaphore("sVS"))
        semST0 = _e(nc.semaphore("sST0"))
        semST1 = _e(nc.semaphore("sST1"))
        semT = _e(nc.semaphore("sT"))
        semV = _e(nc.semaphore("sV"))
        semA = _e(nc.semaphore("sA"))
        semG = _e(nc.semaphore("sG"))
        semY = _e(nc.semaphore("sY"))
    if True:
        semd = {"sL0": semL0, "sL1": semL1, "sSX": semSX, "sOX": semOX, "sW": semW,
                "sVS": semVS, "sST0": semST0, "sST1": semST1,
                "sT": semT, "sV": semV, "sA": semA, "sG": semG, "sY": semY}
        psF = [psF0, psF1, psF2, psF3]
        psFn = ["psF0", "psF1", "psF2", "psF3"]
        psT = [psT0, psT1]
        psTn = ["psT0", "psT1"]
        xts = [xt0, xt1]
        xtn = ["xt0", "xt1"]
        ebs = [eb0, eb1]
        ebn = ["eb0", "eb1"]
        stgs = [stg0, stg1]
        stgn = ["stg0", "stg1"]
        ybs = [yb0, yb1]
        ybn = ["ybA", "ybB"]
        yTs = [yT0, yT1]
        yTn = [["yTA%d" % dc for dc in range(16)], ["yTB%d" % dc for dc in range(16)]]

        SSQ, SSUM, MU, VAR, ISG, NMS, TRCP, SBAR = range(8)

        p.dma(idb_s[:, :], idb_d, writes=["idb"])
        p.dma(idf_s[:, :], idf_d, writes=["idf"])
        p.dma(mask_s[:, :], mask_d, writes=["mask"])
        p.dma(onej_s[:, :], onej_d, writes=["onej"])
        p.dma(bm16_s[:, :], bm16_d, writes=["bm16"])
        p.dma(hind_s[:, :], hind_d, writes=["hind"])
        p.dma(exp16_s[:, :], exp16_d, writes=["exp16"])
        p.dma(XTR3[1:3, :], irb_d, writes=["XTR3a"])
        p.dma(XTR3b[1:3, :], irb_d, writes=["XTR3a2"])
        p.dma(sap(vesb, 0, [[3, 128], [1, 1]]), ve0_d, writes=["ve0"])
        p.dma(sap(vesb, 1, [[3, 128], [1, 1]]), ve1_d, writes=["ve1"])
        p.dma(sap(taur_s, 0, [[1, 128], [1, 1]]), taur_d, writes=["taur"])
        p.dma(sap(sw16, 0, [[16, 128], [1, 16]]),
              sap(sw_d, 0, [[1, 128], [128, 16]]), writes=["sw16"], slow=True)
        p.dma(sap(denf, 0, [[2048, 1], [1, 2048]]),
              sap(bias_d, 0, [[2048, 1], [1, 2048]]), writes=["denf"])

        p.emit("vector", lambda e: e.reciprocal(stt[:, TRCP:TRCP + 1], taur_s[:, 0:1]),
               reads=["taur"], writes=["stt"], hazard=True)
        p.emit("vector", lambda e: e.tensor_scalar(stt[:, SBAR:SBAR + 1], taur_s[:, 0:1], 16.0, None, Alu.mult),
               reads=["taur"], writes=["stt"])
        p.emit("vector", lambda e: e.reciprocal(stt[:, SBAR:SBAR + 1], stt[:, SBAR:SBAR + 1]),
               reads=["stt"], writes=["stt"], hazard=True)
        p.emit("vector", lambda e: e.tensor_tensor(vesb[:, 2:3], vesb[:, 1:2], vesb[:, 0:1], Alu.subtract),
               reads=["ve0", "ve1"], writes=["dv"])
        p.emit("vector", lambda e: e.tensor_scalar(hbrow[0:1, :], denf[0:1, :], 0.5, None, Alu.mult),
               reads=["denf"], writes=["hbrow"])
        p.emit("vector", lambda e: e.tensor_copy(ve0b[:, 0:1], vesb[:, 0:1]),
               reads=["ve0"], writes=["ve0b"])

        p.dma(sap(xsq, 0, [[2048, 128], [128, 16], [1, 128]]),
              sap(Wo_d, 0, [[128, 128], [16384, 16], [1, 128]]), writes=["xsq", "denf"])
        for oc in range(16):
            r = oc % 2
            p.emit("tensor", lambda e, oc=oc, r=r: e.matmul(psT[r][:, 0:128], xsq[:, oc * 128:(oc + 1) * 128], idf_s[:, :], start=True, stop=True),
                   reads=["xsq", "idf"], writes=[psTn[r]])
            p.emit("vector", lambda e, oc=oc, r=r: e.tensor_scalar(WOD[:, oc * 128:(oc + 1) * 128], psT[r][:, 0:128], vesb[:, 2:3], None, Alu.mult),
                   reads=[psTn[r], "dv"], writes=["WOD"])
            p.emit("vector", lambda e, oc=oc, r=r: e.tensor_copy(yT0[:, oc * 128:(oc + 1) * 128], psT[r][:, 0:128]),
                   reads=[psTn[r]], writes=[yTn[0][oc]])
        for ic in range(4):
            p.emit("tensor", lambda e, ic=ic: e.matmul(psQ[0:1, 0:512], ve0b[:, 0:1], yT0[:, ic * 512:(ic + 1) * 512], start=True, stop=True),
                   reads=["ve0b"] + [yTn[0][4 * ic + k] for k in range(4)], writes=["psQ"])
            p.emit("vector", lambda e, ic=ic: e.tensor_copy(w0row[0:1, ic * 512:(ic + 1) * 512], psQ[0:1, 0:512]),
                   reads=["psQ"], writes=["w0row"])

        for oc in range(16):
            p.dma(xsq[:, :], Wf_d[oc * 128:(oc + 1) * 128, :], writes=["xsq", "denf"])
            p.emit("vector", lambda e, oc=oc: e.tensor_scalar(yb0[:, :], xsq[:, :], sw16[:, oc:oc + 1], 0.5, Alu.mult, Alu.mult),
                   reads=["xsq", "sw16"], writes=["ybA"])
            for dc in range(16):
                r = dc % 2
                p.emit("tensor", lambda e, dc=dc, r=r: e.matmul(psT[r][:, 0:128], yb0[:, dc * 128:(dc + 1) * 128], idb_s[:, :], start=True, stop=True),
                       reads=["ybA", "idb"], writes=[psTn[r]])
                p.emit("vector", lambda e, oc=oc, dc=dc, r=r: e.tensor_copy(WT[:, dc * 2048 + oc * 128: dc * 2048 + (oc + 1) * 128], psT[r][:, 0:128]),
                       reads=[psTn[r]], writes=["WT"])

        p.dma(xsq[:, :], Wq_d, writes=["xsq", "denf"])
        for dc in range(16):
            r = dc % 2
            p.emit("tensor", lambda e, dc=dc, r=r: e.matmul(psT[r][:, 0:128], xsq[:, dc * 128:(dc + 1) * 128], idf_s[:, :], start=True, stop=True),
                   reads=["xsq", "idf"], writes=[psTn[r]])
            p.emit("vector", lambda e, dc=dc, r=r: e.tensor_copy(WQKV[:, dc * 192: dc * 192 + 128], psT[r][:, 0:128]),
                   reads=[psTn[r]], writes=["WQKV"])
        p.dma(xsq[0:32, :], Wk_d, writes=["xsq", "xsqk", "denf"])
        for dc in range(16):
            r = dc % 2
            p.emit("tensor", lambda e, dc=dc, r=r: e.matmul(psT[r][0:128, 0:32], xsq[0:32, dc * 128:(dc + 1) * 128], idf_s[0:32, 0:32], start=True, stop=True),
                   reads=["xsqk", "idf"], writes=[psTn[r]])
            p.emit("vector", lambda e, dc=dc, r=r: e.tensor_copy(WQKV[:, dc * 192 + 128: dc * 192 + 160], psT[r][0:128, 0:32]),
                   reads=[psTn[r]], writes=["WQKV"])
        p.dma(xsq[0:32, :], Wv_d, writes=["xsqk", "xsqv", "denf"])
        for dc in range(16):
            r = dc % 2
            p.emit("tensor", lambda e, dc=dc, r=r: e.matmul(psT[r][0:128, 0:32], xsq[0:32, dc * 128:(dc + 1) * 128], idf_s[0:32, 0:32], start=True, stop=True),
                   reads=["xsqv", "idf"], writes=[psTn[r]])
            p.emit("vector", lambda e, dc=dc, r=r: e.tensor_copy(WQKV[:, dc * 192 + 160: dc * 192 + 192], psT[r][0:128, 0:32]),
                   reads=[psTn[r]], writes=["WQKV"])

        p.emit("gpsimd", lambda e: e.memset(qTp[:, 0:31], 0.0), writes=["qTp_pad"])
        p.emit("gpsimd", lambda e: e.memset(kTp[:, 0:31], 0.0), writes=["kTp_pad"])

        def stats_chain(xres, xt, save_jt=None):
            p.emit("scalar", lambda e, xt=xt: e.activation(xsq[:, :], xt[:, :], AF.Square, accum_out=stt[:, SSQ:SSQ + 1]),
                   reads=[xres], writes=["xsq", "xsqk", "xsqv", "denf", "ssq"])
            p.emit("vector", lambda e, xt=xt: e.reduce_sum(stt[:, SSUM:SSUM + 1], xt[:, :], axis=AX.X),
                   reads=[xres], writes=["ssum"], hazard=True)
            p.emit("vector", lambda e: e.tensor_scalar(stt[:, MU:MU + 1], stt[:, SSUM:SSUM + 1], 1.0 / D, None, Alu.mult),
                   reads=["ssum"], writes=["mu"])
            p.emit("vector", lambda e: e.tensor_scalar(stt[:, NMS:NMS + 1], stt[:, MU:MU + 1], stt[:, MU:MU + 1], -EPS, Alu.mult, Alu.add),
                   reads=["mu"], writes=["nms"])
            p.emit("vector", lambda e: e.scalar_tensor_tensor(stt[:, VAR:VAR + 1], stt[:, SSQ:SSQ + 1], 1.0 / D, stt[:, NMS:NMS + 1], Alu.mult, Alu.subtract),
                   reads=["ssq", "nms"], writes=["var"])
            p.emit("scalar", lambda e: e.activation(stt[:, ISG:ISG + 1], stt[:, VAR:VAR + 1], AF.Sqrt),
                   reads=["var"], writes=["sig"])
            p.emit("vector", lambda e: e.reciprocal(stt[:, ISG:ISG + 1], stt[:, ISG:ISG + 1]),
                   reads=["sig"], writes=["isg"], hazard=True)
            if save_jt is not None:
                p.emit("vector", lambda e, jt=save_jt: e.tensor_copy(stbank[:, 2 * jt:2 * jt + 1], stt[:, MU:MU + 1]),
                       reads=["mu"], writes=["stbank"])
                p.emit("vector", lambda e, jt=save_jt: e.tensor_copy(stbank[:, 2 * jt + 1:2 * jt + 2], stt[:, ISG:ISG + 1]),
                       reads=["isg"], writes=["stbank"])

        def normalize(xres, xt, par):
            p.emit("vector", lambda e, xt=xt, par=par: e.tensor_scalar(ybs[par][:, :], xt[:, :], stt[:, MU:MU + 1], stt[:, ISG:ISG + 1], Alu.subtract, Alu.mult),
                   reads=[xres, "mu", "isg"], writes=[ybn[par]])

        def transposes16(par):
            for dc in range(16):
                r = dc % 2
                p.emit("tensor", lambda e, dc=dc, r=r, par=par: e.matmul(psT[r][:, 0:128], ybs[par][:, dc * 128:(dc + 1) * 128], idb_s[:, :], start=True, stop=True),
                       reads=[ybn[par], "idb"], writes=[psTn[r]])
                if dc % 2 == 0:
                    p.emit("vector", lambda e, dc=dc, r=r, par=par: e.tensor_copy(yTs[par][:, dc * 128:(dc + 1) * 128], psT[r][:, 0:128]),
                           reads=[psTn[r]], writes=[yTn[par][dc]])
                else:
                    p.emit("scalar", lambda e, dc=dc, r=r, par=par: e.copy(yTs[par][:, dc * 128:(dc + 1) * 128], psT[r][:, 0:128]),
                           reads=[psTn[r]], writes=[yTn[par][dc]])

        def ffn_mms(par):
            for dc in range(16):
                def mmgrp(e, dc=dc, par=par):
                    inst = None
                    for o4 in range(4):
                        inst = e.matmul(psF[o4][:, 0:512], yTs[par][:, dc * 128:(dc + 1) * 128],
                                        WT[:, dc * 2048 + o4 * 512: dc * 2048 + (o4 + 1) * 512],
                                        start=(dc == 0), stop=False)
                    return inst
                p.emit("tensor", mmgrp, reads=[yTn[par][dc], "WT"], writes=psFn)

            def mmbias(e):
                inst = None
                for o4 in range(4):
                    inst = e.matmul(psF[o4][:, 0:512], onej_s[0:1, 0:128],
                                    hbrow[0:1, o4 * 512:(o4 + 1) * 512], start=False, stop=True)
                return inst
            p.emit("tensor", mmbias, reads=["onej", "hbrow"], writes=psFn)

        def evac_residual(xres, xt):
            for o4 in range(4):
                p.emit("vector", lambda e, o4=o4, xt=xt: e.tensor_tensor(xt[:, o4 * 512:(o4 + 1) * 512], xt[:, o4 * 512:(o4 + 1) * 512], psF[o4][:, 0:512], Alu.add),
                       reads=[psFn[o4], xres], writes=[xres])

        def emit_pv(g, h, prev):
            r, off, wdt, jt, ic, last = prev
            p.emit("tensor", lambda e, r=r, off=off, wdt=wdt, jt=jt, g=g: e.matmul(
                psQ[0:9, off:512], VST[:, jt * 40 + g * 9: jt * 40 + g * 9 + 9], ebs[r][:, 0:wdt],
                start=(jt == 0), stop=last),
                   reads=["VST", "VSTz", ebn[r]] + ["VST%da" % gg for gg in range(4)] + ["VST%db" % gg for gg in range(4)], writes=["psQ"])
            if last:
                sb = ic % 2
                p.emit("vector", lambda e, sb=sb: e.tensor_copy(stgs[sb][0:9, 0:512], psQ[0:9, 0:512]),
                       reads=["psQ"], writes=[stgn[sb]])
                p.dma(atb[h * 8:(h + 1) * 8, ic * 512:(ic + 1) * 512], stgs[sb][0:8, 0:512],
                      reads=[stgn[sb]], writes=["atb%d_%d" % (h, ic)], sem="sST%d" % sb)
                p.dma(denb[h:h + 1, ic * 512:(ic + 1) * 512], stgs[sb][8:9, 0:512],
                      reads=[stgn[sb]], writes=["denb%d_%d" % (h, ic)], sem="sST%d" % sb)

        for s_ in range(NSEQ):
            s = s_

            def preA(jt):
                b = jt % 2
                xt, xres = xts[b], xtn[b]
                p.dma(xt[:, :], X0[s, jt * 128:(jt + 1) * 128, :], writes=[xres], sem="sL%d" % b)
                stats_chain(xres, xt)
                normalize(xres, xt, b)

            preA(0)
            transposes16(0)
            for jt in range(NT):
                if jt + 1 < NT:
                    preA(jt + 1)
                b = jt % 2
                xt, xres = xts[b], xtn[b]
                ffn_mms(b)
                evac_residual(xres, xt)
                p.emit("sync", lambda e, jt=jt, xt=xt: e.dma_start(out=x1scr[jt * 128:(jt + 1) * 128, :], in_=xt[:, :]),
                       reads=[xres], writes=["x1scr%d" % jt], sem="sSX")
                stats_chain(xres, xt, save_jt=jt)
                if jt + 1 < NT:
                    transposes16((jt + 1) % 2)

            def preB(jt):
                b = jt % 2
                xt, xres = xts[b], xtn[b]
                p.dma(xt[:, :], x1scr[jt * 128:(jt + 1) * 128, :],
                      reads=["x1scr%d" % jt], writes=[xres], sem="sL%d" % b,
                      extra=[("sSX", p.cnt["sSX"], "x")])
                p.emit("vector", lambda e, jt=jt, xt=xt, b=b: e.tensor_scalar(ybs[b][:, :], xt[:, :], stbank[:, 2 * jt:2 * jt + 1], stbank[:, 2 * jt + 1:2 * jt + 2], Alu.subtract, Alu.mult),
                       reads=[xres, "stbank"], writes=[ybn[b]])

            def qkvB(jt):
                b = jt % 2
                co = 192 * b
                for dc in range(16):
                    p.emit("tensor", lambda e, dc=dc, b=b: e.matmul(psQ[:, 0:192], yTs[b][:, dc * 128:(dc + 1) * 128], WQKV[:, dc * 192:(dc + 1) * 192], start=(dc == 0), stop=(dc == 15)),
                           reads=[yTn[b][dc], "WQKV"], writes=["psQ"])
                p.emit("scalar", lambda e, co=co: e.activation(qkvst[:, co:co + 160], psQ[:, 0:160], AF.Tanh, scale=stt[:, TRCP:TRCP + 1]),
                       reads=["psQ", "stt"], writes=["qk%d" % b])
                p.emit("scalar", lambda e, co=co: e.activation(qkvst[:, co + 160:co + 192], psQ[:, 160:192], AF.Sigmoid, scale=stt[:, TRCP:TRCP + 1]),
                       reads=["psQ", "stt"], writes=["vv%d" % b])

            def qkTB(jt):
                b = jt % 2
                co = 192 * b
                p.emit("tensor", lambda e, co=co, r=b: e.matmul(psT[r][:, 0:128], qkvst[:, co:co + 128], idb_s[:, :], start=True, stop=True),
                       reads=["qk%d" % b, "idb"], writes=[psTn[b]])
                p.emit("vector", lambda e, jt=jt, r=b: e.tensor_copy(qTp[:, 31 + jt * 128: 31 + (jt + 1) * 128], psT[r][:, 0:128]),
                       reads=[psTn[b]], writes=["qTp"])
                p.emit("tensor", lambda e, co=co, r=b: e.matmul(psT[1 - r][0:32, 0:128], qkvst[:, co + 128:co + 160], idb_s[:, :], start=True, stop=True),
                       reads=["qk%d" % b, "idb"], writes=[psTn[1 - b]])
                p.emit("vector", lambda e, jt=jt, r=b: e.tensor_copy(kTp[0:32, 31 + jt * 128: 31 + (jt + 1) * 128], psT[1 - r][0:32, 0:128]),
                       reads=[psTn[1 - b]], writes=["kTp"])
                p.emit("vector", lambda e, jt=jt, co=co: e.tensor_copy(vtm[:, jt * 32:(jt + 1) * 32], qkvst[:, co + 160:co + 192]),
                       reads=["vv%d" % b], writes=["vtm"])

            preB(0)
            transposes16(0)
            for jt in range(NT):
                if jt + 1 < NT:
                    preB(jt + 1)
                qkvB(jt)
                qkTB(jt)
                if jt + 1 < NT:
                    transposes16((jt + 1) % 2)

            p.emit("scalar", lambda e: e.activation(yb0[:, :], qTp[:, 31:2079], AF.Abs),
                   reads=["qTp", "qTp_pad"], writes=["ybA"])
            p.emit("gpsimd", lambda e: e.memset(PNG0[:, 0:31], 0.0), writes=["PNG0", "NM"])
            p.emit("gpsimd", lambda e: e.memset(PNG0[:, 2079:2112], 0.0), writes=["PNG0"])
            p.emit("gpsimd", lambda e: e.memset(PNG1[:, 2079:2112], 0.0), writes=["PNG1"])
            for ic in range(4):
                p.emit("tensor", lambda e, ic=ic: e.matmul(psQ[0:16, 0:512], hind_s[:, :], yb0[:, ic * 512:(ic + 1) * 512], start=True, stop=True),
                       reads=["ybA", "hind"], writes=["psQ"])
                p.emit("vector", lambda e, ic=ic: e.tensor_copy(PNG0[0:16, 31 + ic * 512: 31 + (ic + 1) * 512], psQ[0:16, 0:512]),
                       reads=["psQ"], writes=["PNG0"])
            cur, oth = (PNG0, "PNG0"), (PNG1, "PNG1")
            for sh in (1, 2, 4, 8, 16):
                p.emit("vector", lambda e, cur=cur, oth=oth, sh=sh: e.tensor_tensor(oth[0][:, 0:2080], cur[0][:, 0:2080], cur[0][:, sh:2080 + sh], Alu.add),
                       reads=[cur[1]], writes=[oth[1]])
                cur, oth = oth, cur
            p.emit("vector", lambda e, cur=cur: e.scalar_tensor_tensor(NM[0:16, 0:2048], cur[0][0:16, 0:2048], -1.0, bm16_s[0:16, :], Alu.mult, Alu.subtract),
                   reads=[cur[1], "bm16", "PNG0"], writes=["NM", "PNG0"])

            p.emit("gpsimd", lambda e: e.memset(VST[:, :], 1.0), writes=["VST", "VSTz"] + ["VST%da" % g for g in range(4)] + ["VST%db" % g for g in range(4)])
            for g in range(4):
                p.dma(sap(VST, g * 9, [[640, 127], [40, 16], [1, 8]]),
                      sap(vtm, 512 + g * 8, [[512, 127], [32, 16], [1, 8]]),
                      reads=["vtm"], writes=["VST%da" % g], sem="sVS")
                p.dma(sap(VST, 127 * 640 + g * 9, [[640, 1], [40, 15], [1, 8]]),
                      sap(vtm, 32 + g * 8, [[512, 1], [32, 15], [1, 8]]),
                      reads=["vtm"], writes=["VST%db" % g], sem="sVS")
            for g in range(4):
                p.dma(sap(VST, 127 * 640 + 600 + g * 9, [[640, 1], [1, 8]]),
                      sap(qTp, 127 * 2080, [[2080, 1], [1, 8]]),
                      reads=["qTp_pad"], writes=["VSTz"], sem="sVS")

            tiles = [(ic, jt) for ic in range(4) for jt in range(min(4 * ic + 3, NT - 1) + 1)]
            for g in range(4):
                kwb = [kw, kw2][g % 2]
                kn = ["kw%d_%d" % (g % 2, c) for c in range(2)]
                for c in range(2):
                    p.dma(sap(kwb, c * 2048, [[4096, 128], [1, 2048]]),
                          sap(kTp, (g * 8) * 2080 + c * 16, [[2080, 8], [1, 16], [1, 2048]]),
                          reads=["kTp", "kTp_pad"], writes=[kn[c]])
                for h in range(4 * g, 4 * g + 4):
                    qwb = [qw, qw2][h % 2]
                    qn = ["qw%d_%d" % (h % 2, c) for c in range(2)]
                    xtb = [XTR3, XTR3b][h % 2]
                    xn = "XTR3_%d" % (h % 2)
                    xan = "XTR3a" if h % 2 == 0 else "XTR3a2"
                    for c in range(2):
                        p.dma(sap(qwb, c * 2048, [[4096, 128], [1, 2048]]),
                              sap(qTp, (h * 8) * 2080 + c * 16, [[2080, 8], [1, 16], [1, 2048]]),
                              reads=["qTp", "qTp_pad"], writes=[qn[c]])
                    p.dma(xtb[0:1, :], NM[h:h + 1, 0:2048], reads=["NM"], writes=[xn])
                    prev = None
                    for n, (ic, jt) in enumerate(tiles):
                        i0 = max(ic * 512, jt * 128)
                        off = i0 - ic * 512
                        wdt = 512 - off
                        last = (jt == min(4 * ic + 3, NT - 1))
                        r = n % 2

                        def mmsc(e, r=r, jt=jt, i0=i0, wdt=wdt, kwb=kwb, qwb=qwb, xtb=xtb):
                            e.matmul(psT[r][:, 0:wdt], kwb[:, jt * 128:(jt + 1) * 128], qwb[:, i0:i0 + wdt], start=True, stop=False)
                            e.matmul(psT[r][:, 0:wdt], kwb[:, 2048 + jt * 128: 2048 + (jt + 1) * 128], qwb[:, 2048 + i0: 2048 + i0 + wdt], start=False, stop=False)
                            return e.matmul(psT[r][:, 0:wdt], onej_s[0:3, jt * 128:(jt + 1) * 128], xtb[0:3, i0:i0 + wdt], start=False, stop=True)
                        p.emit("tensor", mmsc, reads=kn + qn + ["onej", xn, xan], writes=[psTn[r]])
                        if jt >= 4 * ic:
                            p.emit("vector", lambda e, r=r: e.tensor_tensor(psT[r][:, 0:128], psT[r][:, 0:128], mask_s[:, :], Alu.add),
                                   reads=["mask"], writes=[psTn[r]])
                        p.emit("scalar", lambda e, r=r, wdt=wdt: e.activation(ebs[r][:, 0:wdt], psT[r][:, 0:wdt], AF.Exp, scale=stt[:, SBAR:SBAR + 1]),
                               reads=[psTn[r], "stt"], writes=[ebn[r]])
                        if prev is not None:
                            emit_pv(g, h, prev)
                        prev = (r, off, wdt, jt, ic, last)
                    emit_pv(g, h, prev)

            all_atb = ["atb%d_%d" % (h, ic) for h in range(16) for ic in range(4)]
            all_den = ["denb%d_%d" % (h, ic) for h in range(16) for ic in range(4)]
            p.emit("vector", lambda e: e.tensor_scalar(denf[0:16, :], denb[:, :], 1e-37, None, Alu.add),
                   reads=all_den + ["denf"], writes=["denf"])
            p.emit("vector", lambda e: e.reciprocal(denf[0:16, :], denf[0:16, :]),
                   reads=["denf"], writes=["denf"], hazard=True)
            for ic in range(4):
                p.emit("tensor", lambda e, ic=ic: e.matmul(psF[ic][:, 0:512], exp16_s[:, :], denf[0:16, ic * 512:(ic + 1) * 512], start=True, stop=True),
                       reads=["denf", "exp16"], writes=[psFn[ic]])
                p.emit("vector", lambda e, ic=ic: e.tensor_tensor(atf[:, ic * 512:(ic + 1) * 512], atb[:, ic * 512:(ic + 1) * 512], psF[ic][:, 0:512], Alu.mult),
                       reads=all_atb + [psFn[ic]], writes=["atf"])

            prB = [psQ, psT0, psT1, psQ]
            prBn = ["psQ", "psT0", "psT1", "psQ"]

            def preC(jt):
                b = jt % 2
                xt, xres = xts[b], xtn[b]
                p.dma(xt[:, :], x1scr[jt * 128:(jt + 1) * 128, :],
                      reads=["x1scr%d" % jt], writes=[xres], sem="sL%d" % b,
                      extra=[("sSX", p.cnt["sSX"], "x")])
                def pj(e, jt=jt, o4s=(0, 1, 2)):
                    inst = None
                    for o4 in o4s:
                        inst = e.matmul(prB[o4][:, 0:512], atf[:, jt * 128:(jt + 1) * 128], WOD[:, o4 * 512:(o4 + 1) * 512], start=True, stop=False)
                    for o4 in o4s:
                        inst = e.matmul(prB[o4][:, 0:512], onej_s[0:1, jt * 128:(jt + 1) * 128], w0row[0:1, o4 * 512:(o4 + 1) * 512], start=False, stop=True)
                    return inst
                p.emit("tensor", lambda e, jt=jt: pj(e, jt, (0, 1, 2)), reads=["atf", "WOD", "onej", "w0row"], writes=["psQ", "psT0", "psT1"])
                for o4 in (0, 1, 2):
                    p.emit("vector", lambda e, o4=o4, xt=xt: e.tensor_tensor(xt[:, o4 * 512:(o4 + 1) * 512], xt[:, o4 * 512:(o4 + 1) * 512], prB[o4][:, 0:512], Alu.add),
                           reads=[prBn[o4], xres], writes=[xres])
                p.emit("tensor", lambda e, jt=jt: pj(e, jt, (3,)), reads=["atf", "WOD", "onej", "w0row"], writes=["psQ"])
                p.emit("vector", lambda e, xt=xt: e.tensor_tensor(xt[:, 3 * 512:4 * 512], xt[:, 3 * 512:4 * 512], psQ[:, 0:512], Alu.add),
                       reads=["psQ", xres], writes=[xres])
                stats_chain(xres, xt)
                normalize(xres, xt, b)

            preC(0)
            transposes16(0)
            for jt in range(NT):
                if jt + 1 < NT:
                    preC(jt + 1)
                b = jt % 2
                xt, xres = xts[b], xtn[b]
                ffn_mms(b)
                evac_residual(xres, xt)
                p.emit("sync", lambda e, s=s, jt=jt, xt=xt: e.dma_start(out=OUT[s, jt * 128:(jt + 1) * 128, :], in_=xt[:, :]),
                       reads=[xres], writes=["out%d_%d" % (s, jt)], sem="sOX")
                if jt + 1 < NT:
                    transposes16((jt + 1) % 2)

        p.emit("sync", lambda e: e.nop(),
               extra=[(sm, p.cnt[sm], "x") for sm in DMASEMS])

        with nc.Block() as blk:
            @blk.sync
            def _(eng):
                for f in p.q["sync"]:
                    f(eng, semd)

            @blk.tensor
            def _(eng):
                for f in p.q["tensor"]:
                    f(eng, semd)

            @blk.vector
            def _(eng):
                for f in p.q["vector"]:
                    f(eng, semd)

            @blk.scalar
            def _(eng):
                for f in p.q["scalar"]:
                    f(eng, semd)

            @blk.gpsimd
            def _(eng):
                for f in p.q["gpsimd"]:
                    f(eng, semd)

    return nc


def _consts():
    j = np.arange(S, dtype=np.float64)
    jhi = (j.astype(np.int64) & ~255).astype(np.float64)
    jlo = (j.astype(np.int64) & 255).astype(np.float64)
    onej = np.stack([np.ones(S), jhi, jlo]).astype(BF)
    ir = 16.0 / (j + 1.0)
    irb = np.stack([ir, ir]).astype(BF)
    bm16 = np.broadcast_to(16.0 * (j - 1.0) / (j + 1.0), (16, S)).astype(BF)
    idb = np.eye(128, dtype=BF)
    idf = np.eye(128, dtype=np.float32)
    pp, cc = np.mgrid[0:128, 0:128]
    maskt = np.where(pp >= cc, np.float32(NEG), np.float32(0.0)).astype(np.float32)
    hind = np.zeros((128, 16), dtype=BF)
    for h in range(16):
        hind[h * 8:(h + 1) * 8, h] = 1
    exp16 = np.zeros((16, 128), dtype=np.float32)
    for h in range(16):
        exp16[h, h * 8:(h + 1) * 8] = 1
    return dict(onej=onej, irb=irb, bm16=bm16, idb=idb, idf=idf, maskt=maskt,
                hind=hind, exp16=exp16)


def kernel(L0_attn_hidden_states, L0_attn_Wq, L0_attn_Wk, L0_attn_Wv, L0_attn_Wo,
           L0_attn_v_emb0, L0_attn_v_emb1, L0_attn_tau,
           L0_ffn_weight, L0_ffn_scale_weight, L0_ffn_bias):
    x = np.ascontiguousarray(np.asarray(L0_attn_hidden_states, dtype=np.float32))
    consts = _consts()
    taur = np.broadcast_to(np.asarray(L0_attn_tau, dtype=np.float32).reshape(1), (128,)).copy()
    shared = dict(
        Wq=np.ascontiguousarray(L0_attn_Wq, dtype=np.float32),
        Wk=np.ascontiguousarray(L0_attn_Wk, dtype=np.float32),
        Wv=np.ascontiguousarray(L0_attn_Wv, dtype=np.float32),
        Wo=np.ascontiguousarray(L0_attn_Wo, dtype=np.float32),
        Wf=np.ascontiguousarray(L0_ffn_weight, dtype=np.float32),
        sw=np.ascontiguousarray(L0_ffn_scale_weight, dtype=np.float32),
        bias=np.ascontiguousarray(L0_ffn_bias, dtype=np.float32),
        ve0=np.ascontiguousarray(L0_attn_v_emb0, dtype=np.float32),
        ve1=np.ascontiguousarray(L0_attn_v_emb1, dtype=np.float32),
        taur=taur,
        **consts,
    )
    nc = build_program()
    in_maps = []
    for c in range(8):
        m = dict(shared)
        m["X0"] = x[c * NSEQ:(c + 1) * NSEQ]
        in_maps.append(m)
    res = run_bass_kernel_spmd(nc, in_maps, core_ids=list(range(8)))
    out = np.concatenate([r["OUT"] for r in res.results], axis=0)
    return out


